# version 18
# speedup vs baseline: 1.3769x; 1.0312x over previous
import contextlib
import numpy as np

import concourse.bass as bass
import concourse.mybir as mybir
from concourse.bass_utils import run_bass_kernel_spmd

F32 = mybir.dt.float32
BF16 = mybir.dt.bfloat16
F8 = mybir.dt.float8e4
AF = mybir.ActivationFunctionType
ALU = mybir.AluOpType
PM = mybir.MatmulPerfMode

B, L, D, DI = 4, 2048, 2048, 256
DH = 8192
OWN, HALO = 1024, 256
EXT = OWN + 2 * HALO
NT_EXT = EXT // 128
NT_OWN = OWN // 128
EPS = 1e-5
GELU_C0 = 0.044715
GELU_C1 = 0.7978845608028654

ENGINES = ("sync", "scalar", "vector", "gpsimd", "tensor")

_DEBUG_TAPS = []
_PHASE_LIMIT = 99
_A1_TILES = 99
_A1_CUT = 99
_A4_SEQ = False
_N_CORES = 8


class Sched:

    def __init__(self, nc, stack):
        self.nc = nc
        self.stack = stack
        self.q = {e: [] for e in ENGINES}
        self.sem = {}
        self.cnt = {}
        self.mute = False

    def new_sem(self, name):
        self.sem[name] = self.stack.enter_context(self.nc.semaphore(name))
        self.cnt[name] = 0

    SELF_SEM = {"vector": "dve", "scalar": "act", "gpsimd": "gps"}

    def do(self, eng, fn, wait=(), inc=None, inc_amt=1):
        wait = list(wait)
        self_sem = self.SELF_SEM.get(eng)
        if self_sem is not None and inc is None:
            inc = self_sem
        if self_sem is not None and inc == self_sem:
            wait.append((self_sem, self.cnt[self_sem]))
        if self.mute:
            return self.cnt[inc] if inc is not None else None
        post = None
        if inc is not None:
            self.cnt[inc] += inc_amt
            post = self.cnt[inc]
        sems = self.sem
        seen = set()
        wl = []
        for nm, v in wait:
            if v > 0 and (nm, v) not in seen:
                seen.add((nm, v))
                wl.append((nm, v))

        def run(e, fn=fn, wl=wl, inc=inc, inc_amt=inc_amt):
            for nm, v in wl:
                e.wait_ge(sems[nm], v)
            ins = fn(e)
            if inc is not None:
                ins.then_inc(sems[inc], inc_amt)
        self.q[eng].append(run)
        return post

    def wait_only(self, eng, wait):
        if self.mute:
            return
        wl = [(nm, v) for (nm, v) in wait if v > 0]
        sems = self.sem

        def run(e, wl=wl):
            for nm, v in wl:
                e.wait_ge(sems[nm], v)
        self.q[eng].append(run)

    def cur(self, name):
        return self.cnt[name]

    BAR_SELF = {"vector": "dve", "scalar": "act", "tensor": "pe",
                "gpsimd": "gps"}

    def barrier(self, engines=ENGINES):
        marks = {}
        for e in engines:
            w = []
            if e in self.BAR_SELF:
                w.append((self.BAR_SELF[e], self.cnt[self.BAR_SELF[e]]))
            marks[e] = self.do(e, lambda h: h.nop(), wait=w, inc="bar_" + e)
        eng_sems = set(self.BAR_SELF.values())
        dma_waits = [(s, self.cnt[s]) for s in self.cnt
                     if s not in eng_sems and not s.startswith("bar_")
                     and self.cnt[s] > 0]
        for e in engines:
            self.wait_only(e, [("bar_" + o, marks[o])
                               for o in engines if o != e] + dma_waits)


def build(debug_taps=()):
    nc = bass.Bass("TRN2", target_bir_lowering=False)
    dbg = set(debug_taps)

    xd = nc.dram_tensor("x", [EXT, D], F32, kind="ExternalInput").ap()
    win_d = nc.dram_tensor("w_in", [D, DI], F32, kind="ExternalInput").ap()
    wf_d = nc.dram_tensor("w_fwd", [DI, 2 * DI], F32, kind="ExternalInput").ap()
    wb_d = nc.dram_tensor("w_bwd", [DI, 2 * DI], F32, kind="ExternalInput").ap()
    wo_d = nc.dram_tensor("w_out", [2 * DI, D], F32, kind="ExternalInput").ap()
    gam_d = nc.dram_tensor("gamma", [D], F32, kind="ExternalInput").ap()
    bet_d = nc.dram_tensor("beta", [D], F32, kind="ExternalInput").ap()
    w0_d = nc.dram_tensor("w0", [D, DH], F32, kind="ExternalInput").ap()
    w1_d = nc.dram_tensor("w1", [D, DH], F32, kind="ExternalInput").ap()
    w2_d = nc.dram_tensor("w2", [DH, D], F32, kind="ExternalInput").ap()
    yd = nc.dram_tensor("y", [OWN, D], F32, kind="ExternalOutput").ap()
    s1_kind = "ExternalOutput" if "s1" in dbg else "Internal"
    s1_d = nc.dram_tensor("s1", [OWN, D], F32, kind=s1_kind).ap()
    u1_d = nc.dram_tensor("u1sc", [OWN, D], F32, kind="Internal").ap()

    taps = {}

    def tap(name, shape, dtype=F32):
        if name in dbg:
            taps[name] = nc.dram_tensor("tap_" + name, list(shape), dtype,
                                        kind="ExternalOutput").ap()
            return taps[name]
        return None

    with contextlib.ExitStack() as stack:
        S = Sched(nc, stack)
        for s in ("pe", "act", "dve", "gps",
                  "sx0", "sx1", "swin", "swfb", "swo", "sgb",
                  "ss1w0", "ss1w1", "su10", "su11", "su1r0", "su1r1",
                  "sb0", "sb1", "sto0", "sto1",
                  "swl0a", "swl0b", "swl1a", "swl1b", "swl2a", "swl2b",
                  "stap",
                  "sw2p", "sw200", "sw201", "sw202", "sw203",
                  "sw210", "sw211", "sw212", "sw213"):
            S.new_sem(s)
        for e in ENGINES:
            S.new_sem("bar_" + e)

        def sb(name, shape, dtype, side="left"):
            return stack.enter_context(
                nc.sbuf_tensor(name, list(shape), dtype, side=side))

        ps = [stack.enter_context(nc.psum_tensor(f"ps{i}", [128, 512], F32))
              for i in range(8)]
        bank_ev = [(None, 0)] * 8

        ident32 = sb("ident32", (128, 128), F32)
        iot = sb("iot", (128, 128), F32)
        scratch_ref = [None]
        NSTAT = 16
        st_ssum = sb("st_ssum", (128, NSTAT), F32)
        st_ssq = sb("st_ssq", (128, NSTAT), F32)
        st_mean = sb("st_mean", (128, NSTAT), F32)
        st_varp = sb("st_varp", (128, NSTAT), F32)
        st_rstd = sb("st_rstd", (128, NSTAT), F32)
        st_nb = sb("st_nb", (128, NSTAT), F32)
        st_t1 = sb("st_t1", (128, NSTAT), F32)

        scratch_ref[0] = stack.enter_context(
            nc.sbuf_tensor("scratch", [128, D], F32, side="right"))
        w2pre = stack.enter_context(
            nc.sbuf_tensor("w2pre", [128, 8 * 2 * 512], F8, side="right"))
        w2pre4 = w2pre.ap().rearrange("p (kp i n) -> p kp i n", kp=8, i=2)
        u2q_stack = contextlib.ExitStack()
        u2q = u2q_stack.enter_context(
            nc.sbuf_tensor("u2q", [128, 8 * 2 * OWN], F8, side="right"))
        u2q4 = u2q.ap().rearrange("p (kp i t) -> p kp i t", kp=8, i=2)

        S.do("gpsimd", lambda g: g.iota(iot[:], pattern=[[1, 128]], base=0,
                                        channel_multiplier=-1,
                                        allow_small_or_imprecise_dtypes=True),
             inc="gps")
        S.do("vector", lambda v: v.tensor_scalar(
            out=ident32[:], in0=iot[:], scalar1=0.0, scalar2=None,
            op0=ALU.is_equal), wait=[("gps", S.cur("gps"))], inc="dve")
        ident_done = S.cur("dve")

        def pe_tick(bank, wait):
            w = list(wait) + [("dve", ident_done)]
            if bank_ev[bank][0]:
                w.append((bank_ev[bank][0], bank_ev[bank][1]))
            S.do("tensor", lambda t, bank=bank: t.transpose(
                ps[bank][:, 0:128], ident32[:], ident32[:]), wait=w, inc="pe")

        ln_ticks = []

        S.mute = S.mute or _PHASE_LIMIT < 0
        def dve_chain(steps, wait=()):
            first = True
            for fn in steps:
                w = list(wait) if first else [("dve", S.cur("dve"))]
                first = False
                S.do("vector", fn, wait=w, inc="dve")

        def ln_stats(col, src_ap, wait, d=D, newton=True, sum_done=None,
                     mean_shift=None):
            c = slice(col, col + 1)
            scratch = scratch_ref[0]
            sq_q = S.do("scalar", lambda a, scratch=scratch: a.activation(
                out=scratch[:, 0:d], in_=src_ap, func=AF.Square,
                accum_out=st_ssq[:, c]), wait=wait, inc="act")
            ln_ticks[:] = [sq_q]
            if sum_done is None:
                aq = S.do("scalar", lambda a, scratch=scratch: a.activation(
                    out=scratch[:, 0:d], in_=src_ap, func=AF.Identity,
                    accum_out=st_ssum[:, c]), inc="act")
                ln_ticks.append(aq)
                w0 = [("act", aq)]
            else:
                w0 = [("act", sq_q), ("dve", sum_done)]
            _ln_chain(col, d, newton, w0, mean_shift)
            return S.cur("dve")

        def _ln_chain(col, d, newton, w0, mean_shift=None):
            c = slice(col, col + 1)
            if mean_shift is None:
                mean_op = (lambda v: v.tensor_scalar_mul(
                    out=st_mean[:, c], in0=st_ssum[:, c], scalar1=1.0 / d))
            else:
                mean_op = (lambda v: v.scalar_tensor_tensor(
                    out=st_mean[:, c], in0=st_ssum[:, c], scalar=1.0 / d,
                    in1=mean_shift, op0=ALU.mult, op1=ALU.add))
            dve_chain([
                mean_op,
                lambda v: v.scalar_tensor_tensor(out=st_varp[:, c], in0=st_mean[:, c],
                                                 scalar=-1.0, in1=st_mean[:, c],
                                                 op0=ALU.mult, op1=ALU.mult),
                lambda v: v.scalar_tensor_tensor(out=st_varp[:, c], in0=st_ssq[:, c],
                                                 scalar=1.0 / d, in1=st_varp[:, c],
                                                 op0=ALU.mult, op1=ALU.add),
                lambda v: v.tensor_scalar_add(out=st_varp[:, c], in0=st_varp[:, c],
                                              scalar1=EPS),
            ], wait=w0)
            sq = S.do("scalar", lambda a: a.activation(
                out=st_rstd[:, c], in_=st_varp[:, c], func=AF.Sqrt),
                wait=[("dve", S.cur("dve"))], inc="act")
            ln_ticks.append(sq)
            steps = [lambda v: v.reciprocal(out=st_rstd[:, c], in_=st_rstd[:, c])]
            if newton:
                steps += [
                    lambda v: v.tensor_tensor(out=st_t1[:, c], in0=st_rstd[:, c],
                                              in1=st_rstd[:, c], op=ALU.mult),
                    lambda v: v.tensor_tensor(out=st_t1[:, c], in0=st_t1[:, c],
                                              in1=st_varp[:, c], op=ALU.mult),
                    lambda v: v.tensor_scalar(out=st_t1[:, c], in0=st_t1[:, c],
                                              scalar1=-0.5, scalar2=1.5,
                                              op0=ALU.mult, op1=ALU.add),
                    lambda v: v.tensor_tensor(out=st_rstd[:, c], in0=st_rstd[:, c],
                                              in1=st_t1[:, c], op=ALU.mult),
                ]
            steps += [
                lambda v: v.scalar_tensor_tensor(out=st_nb[:, c], in0=st_mean[:, c],
                                                 scalar=-1.0, in1=st_rstd[:, c],
                                                 op0=ALU.mult, op1=ALU.mult),
            ]
            dve_chain(steps, wait=[("act", sq)])
            return S.cur("dve")

        def ln_stats_post(col, d=D, newton=True):
            _ln_chain(col, d, newton, [("dve", S.cur("dve"))])
            return S.cur("dve")

        def ln_apply(col, out_ap, src_ap, wait_extra=()):
            c = slice(col, col + 1)
            return S.do("scalar", lambda a: a.activation(
                out=out_ap, in_=src_ap, func=AF.Identity,
                bias=st_nb[:, c], scale=st_rstd[:, c]),
                wait=list(wait_extra), inc="act")

        with contextlib.ExitStack() as pa:
            def sba(name, shape, dtype):
                return pa.enter_context(nc.sbuf_tensor(name, list(shape), dtype))

            hh = sba("hh", (128, 4 * OWN), BF16)
            hl = sba("hl", (128, 4 * OWN), BF16)
            hh3 = hh.ap().rearrange("p (c t) -> p c t", c=4)
            hl3 = hl.ap().rearrange("p (c t) -> p c t", c=4)
            wo_h = sba("wo_h", (128, 4 * D), BF16)
            wo_l = sba("wo_l", (128, 4 * D), BF16)
            wo_h3 = wo_h.ap().rearrange("p (c n) -> p c n", c=4)
            wo_l3 = wo_l.ap().rearrange("p (c n) -> p c n", c=4)

            with contextlib.ExitStack() as pz:
                def sbz(name, shape, dtype):
                    return pz.enter_context(nc.sbuf_tensor(name, list(shape), dtype))

                z32 = [sbz(f"z32_{m}", (128, EXT), F32) for m in range(2)]
                z_h = [sbz(f"zh_{m}", (128, EXT), BF16) for m in range(2)]
                z_l = [sbz(f"zl_{m}", (128, EXT), BF16) for m in range(2)]
                wfb_h = sbz("wfb_h", (128, 2 * 2 * 2 * DI), BF16)
                wfb_l = sbz("wfb_l", (128, 2 * 2 * 2 * DI), BF16)
                wfb_h4 = wfb_h.ap().rearrange("p (d kc m) -> p d kc m", d=2, kc=2)
                wfb_l4 = wfb_l.ap().rearrange("p (d kc m) -> p d kc m", d=2, kc=2)

                with contextlib.ExitStack() as p1:
                    def sb1(name, shape, dtype):
                        return p1.enter_context(
                            nc.sbuf_tensor(name, list(shape), dtype))

                    stg_win = sb1("stg_win", (128, 16 * DI), F32)
                    stg_wfb = sb1("stg_wfb", (128, 2 * 2 * DI), F32)
                    win_h = sb1("win_h", (128, 16 * DI), BF16)
                    win_l = sb1("win_l", (128, 16 * DI), BF16)
                    win_h3 = win_h.ap().rearrange("p (kc m) -> p kc m", kc=16)
                    win_l3 = win_l.ap().rearrange("p (kc m) -> p kc m", kc=16)

                    dq = S.do("sync", lambda s: s.dma_start(
                        out=stg_win[:].rearrange("p (kc m) -> p kc m", kc=16),
                        in_=win_d.rearrange("(kc p) m -> p kc m", p=128)),
                        inc="swin", inc_amt=16)
                    hq = S.do("scalar", lambda a: a.copy(
                        out=win_h[:], in_=stg_win[:]),
                        wait=[("swin", dq)], inc="act")
                    S.do("vector", lambda v: v.scalar_tensor_tensor(
                        out=win_l[:], in0=win_h[:], scalar=-1.0, in1=stg_win[:],
                        op0=ALU.mult, op1=ALU.add),
                        wait=[("act", hq)], inc="dve")
                    for di, wsrc in enumerate((wf_d, wb_d)):
                        dq = S.do("sync", lambda s, di=di, wsrc=wsrc: s.dma_start(
                            out=stg_wfb[:].rearrange("p (kc m) -> p kc m", kc=2),
                            in_=wsrc.rearrange("(kc p) m -> p kc m", p=128)),
                            wait=[("dve", S.cur("dve"))] if di == 1 else [],
                            inc="swfb", inc_amt=16)
                        hq = S.do("scalar", lambda a, di=di: a.copy(
                            out=wfb_h4[:, di], in_=stg_wfb[:].rearrange(
                                "p (kc m) -> p kc m", kc=2)),
                            wait=[("swfb", dq)], inc="act")
                        S.do("vector", lambda v, di=di: v.scalar_tensor_tensor(
                            out=wfb_l4[:, di], in0=wfb_h4[:, di], scalar=-1.0,
                            in1=stg_wfb[:].rearrange("p (kc m) -> p kc m", kc=2),
                            op0=ALU.mult, op1=ALU.add),
                            wait=[("act", hq)], inc="dve")
                    win_ready = S.cur("dve")

                    xt = [sb1(f"xt{i}", (128, D), F32) for i in range(2)]
                    u1t = [sb1(f"u1t{i}", (128, D), F32) for i in range(2)]
                    ufh = [sb1(f"ufh{i}", (128, D), BF16) for i in range(2)]
                    ufl = [sb1(f"ufl{i}", (128, D), BF16) for i in range(2)]
                    zt = [sb1(f"zt{i}", (128, DI), F32) for i in range(2)]
                    xt_free = [0, 0]
                    u1t_free = [0, 0]
                    u1w_free = [0, 0]
                    uf_free = [0, 0]
                    zt_free = [0, 0]
                    a1_state = {}

                    def a1_pe_work(i, bi, aq, ticks=()):
                        lq = 0
                        for f in range(16):
                            bank = (6, 7, 0, 1)[f // 4]
                            col = (f % 4) * 128
                            pe_w = [("act", aq)] if f == 0 else []
                            if f % 4 == 0 and bank_ev[bank][0]:
                                pe_w = pe_w + [(bank_ev[bank][0], bank_ev[bank][1])]
                            pq = S.do("tensor", lambda t, f=f, bi=bi, bank=bank,
                                      col=col:
                                      t.transpose(ps[bank][:, col:col + 128],
                                                  u1t[bi][:, f * 128:(f + 1) * 128],
                                                  ident32[:]),
                                      wait=pe_w, inc="pe")
                            if f % 4 != 3:
                                continue
                            g0 = (f - 3) * 128
                            hq = S.do("vector", lambda v, g0=g0, bi=bi, bank=bank:
                                      v.tensor_copy(
                                          out=ufh[bi][:, g0:g0 + 512],
                                          in_=ps[bank][:, 0:512]),
                                      wait=[("pe", pq)], inc="dve")
                            lq = S.do("vector", lambda v, g0=g0, bi=bi, bank=bank:
                                      v.scalar_tensor_tensor(
                                          out=ufl[bi][:, g0:g0 + 512],
                                          in0=ufh[bi][:, g0:g0 + 512],
                                          scalar=-1.0, in1=ps[bank][:, 0:512],
                                          op0=ALU.mult, op1=ALU.add),
                                      wait=[("dve", hq)], inc="dve")
                            bank_ev[bank] = ("dve", lq)
                        u1t_free[bi] = S.cur("pe")
                        ufh3 = ufh[bi].ap().rearrange("p (f t) -> p f t", f=16)
                        ufl3 = ufl[bi].ap().rearrange("p (f t) -> p f t", f=16)
                        zbank = 4 + (i % 2)
                        nmm = 0
                        q = None
                        for kc in range(16):
                            for (lh, rh) in ((ufh3, win_h3), (ufh3, win_l3),
                                             (ufl3, win_h3)):
                                nmm += 1
                                pe_w = []
                                if nmm == 1:
                                    pe_w = [("dve", max(lq, win_ready))]
                                    if bank_ev[zbank][0]:
                                        pe_w.append((bank_ev[zbank][0],
                                                     bank_ev[zbank][1]))
                                q = S.do("tensor", lambda t, kc=kc, lh=lh, rh=rh,
                                         zbank=zbank, first=(nmm == 1),
                                         last=(nmm == 48): t.matmul(
                                             ps[zbank][:, 0:DI],
                                             lh[:, kc, :], rh[:, kc, :],
                                             start=first, stop=last),
                                         wait=pe_w, inc="pe")
                        uf_free[bi] = S.cur("pe")
                        a1_state["q"] = q
                        a1_state["zbank"] = zbank

                    def a1_pe_zevac(i, bi):
                        q = a1_state["q"]
                        zbank = a1_state["zbank"]
                        eq = S.do("vector", lambda v, bi=bi, zbank=zbank:
                                  v.tensor_copy(out=zt[bi][:],
                                                in_=ps[zbank][:, 0:DI]),
                                  wait=[("pe", q), ("pe", zt_free[bi])], inc="dve")
                        bank_ev[zbank] = ("dve", eq)
                        for m in range(2):
                            bank = 2 + ((2 * i + m) % 2)
                            pe_w = [("dve", eq)]
                            if bank_ev[bank][0]:
                                pe_w.append((bank_ev[bank][0], bank_ev[bank][1]))
                            pq = S.do("tensor", lambda t, m=m, bi=bi, bank=bank:
                                      t.transpose(ps[bank][:, 0:128],
                                                  zt[bi][:, m * 128:(m + 1) * 128],
                                                  ident32[:]),
                                      wait=pe_w, inc="pe")
                            S.do("vector", lambda v, m=m, i=i, bank=bank:
                                 v.tensor_copy(
                                     out=z32[m][:, i * 128:(i + 1) * 128],
                                     in_=ps[bank][:, 0:128]),
                                 wait=[("pe", pq)], inc="dve")
                            hq2 = S.do("vector", lambda v, m=m, i=i, bank=bank:
                                       v.tensor_copy(
                                           out=z_h[m][:, i * 128:(i + 1) * 128],
                                           in_=ps[bank][:, 0:128]),
                                       inc="dve")
                            lq2 = S.do("vector", lambda v, m=m, i=i, bank=bank:
                                       v.scalar_tensor_tensor(
                                           out=z_l[m][:, i * 128:(i + 1) * 128],
                                           in0=z_h[m][:, i * 128:(i + 1) * 128],
                                           scalar=-1.0, in1=ps[bank][:, 0:128],
                                           op0=ALU.mult, op1=ALU.add),
                                       wait=[("dve", hq2)], inc="dve")
                            bank_ev[bank] = ("dve", lq2)
                        zt_free[bi] = S.cur("pe")

                    pending_pe = None
                    pending_zevac = None
                    for i in range(NT_EXT):
                        S.mute = S.mute or (i >= _A1_TILES)
                        bi = i % 2
                        dq = S.do("sync", lambda s, i=i, bi=bi: s.dma_start(
                            out=xt[bi][:], in_=xd[i * 128:(i + 1) * 128, :]),
                            wait=[("act", xt_free[bi])],
                            inc=f"sx{bi}", inc_amt=16)
                        if pending_pe is not None:
                            pending_pe()
                        ln_stats(i % NSTAT, xt[bi][:],
                                 wait=[(f"sx{bi}", dq)])
                        aq = ln_apply(i % NSTAT, u1t[bi][:], xt[bi][:],
                                      wait_extra=[("dve", S.cur("dve")),
                                                  ("pe", u1t_free[bi]),
                                                  (f"su1{bi}", u1w_free[bi])])
                        xt_free[bi] = aq
                        if 2 <= i < 10:
                            u1w_free[bi] = S.do(
                                "sync", lambda s, i=i, bi=bi: s.dma_start(
                                    out=u1_d[(i - 2) * 128:(i - 1) * 128, :],
                                    in_=u1t[bi][:]),
                                wait=[("act", aq)],
                                inc=f"su1{bi}", inc_amt=16)
                        if pending_zevac is not None:
                            pending_zevac()
                        pending_pe = (lambda i=i, bi=bi, aq=aq:
                                      a1_pe_work(i, bi, aq))
                        pending_zevac = (lambda i=i, bi=bi:
                                         a1_pe_zevac(i, bi))
                    pending_pe()
                    pending_zevac()

                    S.barrier()

                if "z32" in dbg:
                    tap("z32", (2, 128, EXT))
                    for m in range(2):
                        S.do("sync", lambda s, m=m: s.dma_start(
                            out=taps["z32"][m], in_=z32[m][:]),
                            wait=[("dve", S.cur("dve"))], inc="stap", inc_amt=16)
                    S.wait_only("sync", [("stap", S.cur("stap"))])
                S.mute = S.mute or _PHASE_LIMIT < 1

                with contextlib.ExitStack() as p3:
                    def sb3(name, shape, dtype):
                        return p3.enter_context(
                            nc.sbuf_tensor(name, list(shape), dtype))

                    stg_wo = sb3("stg_wo", (128, 4 * D), F32)
                    dqw = S.do("sync", lambda s: s.dma_start(
                        out=stg_wo[:].rearrange("p (c n) -> p c n", c=4),
                        in_=wo_d.rearrange("(c p) n -> p c n", p=128)),
                        inc="swo", inc_amt=16)
                    hqw = S.do("scalar", lambda a: a.copy(
                        out=wo_h[:], in_=stg_wo[:]), wait=[("swo", dqw)],
                        inc="act")
                    S.do("vector", lambda v: v.scalar_tensor_tensor(
                        out=wo_l[:], in0=wo_h[:], scalar=-1.0, in1=stg_wo[:],
                        op0=ALU.mult, op1=ALU.add),
                        wait=[("act", hqw)], inc="dve")

                    h_f = [sb3(f"hf_{m}", (128, EXT), F32) for m in range(2)]
                    h_b = [sb3(f"hb_{m}", (128, EXT), F32) for m in range(2)]
                    delta = [sb3(f"delta_{m}", (128, EXT), F32) for m in range(2)]
                    gate = [sb3(f"gate_{m}", (128, EXT), F32) for m in range(2)]

                    for di, direction in enumerate(("fwd", "bwd")):
                        rev = direction == "bwd"
                        dir_gate = [("dve", S.cur("dve"))] if rev else []
                        htgt = h_f if not rev else h_b
                        for om in range(4):
                            for sl in range(3):
                                t0, t1 = sl * 512, (sl + 1) * 512
                                bank = sl % 2 + (2 if om % 2 else 0)
                                nmm = 0
                                q = None
                                for kc in range(2):
                                    for (lh, rh) in ((wfb_h4, z_h), (wfb_h4, z_l),
                                                     (wfb_l4, z_h)):
                                        nmm += 1
                                        pe_w = []
                                        if nmm == 1 and bank_ev[bank][0]:
                                            pe_w = [(bank_ev[bank][0],
                                                     bank_ev[bank][1])]
                                        q = S.do("tensor", lambda t, kc=kc, lh=lh,
                                                 rh=rh, di=di, om=om, t0=t0, t1=t1,
                                                 bank=bank, first=(nmm == 1),
                                                 last=(nmm == 6): t.matmul(
                                                     ps[bank][:],
                                                     lh[:, di, kc,
                                                        om * 128:(om + 1) * 128],
                                                     rh[kc][:, t0:t1],
                                                     start=first, stop=last),
                                                 wait=pe_w, inc="pe")
                                m = om % 2
                                if om < 2:
                                    eq = S.do("scalar", lambda a, m=m, bank=bank,
                                              t0=t0, t1=t1:
                                              a.activation(
                                                  out=delta[m][:, t0:t1],
                                                  in_=ps[bank][:],
                                                  func=AF.Sigmoid),
                                              wait=[("pe", q)] + dir_gate,
                                              inc="act")
                                    bank_ev[bank] = ("act", eq)
                                else:
                                    eq = S.do("vector", lambda v, m=m, bank=bank,
                                              t0=t0, t1=t1:
                                              v.tensor_tensor(
                                                  out=gate[m][:, t0:t1],
                                                  in0=ps[bank][:],
                                                  in1=z32[m][:, t0:t1],
                                                  op=ALU.mult),
                                              wait=[("pe", q)], inc="dve")
                                    bank_ev[bank] = ("dve", eq)
                        scan_w = [("act", S.cur("act")), ("dve", S.cur("dve"))]
                        for m in range(2):
                            S.do("vector", lambda v, m=m, htgt=htgt, rev=rev:
                                 v.tensor_tensor_scan(
                                     out=(htgt[m][:][:, ::-1] if rev
                                          else htgt[m][:]),
                                     data0=(delta[m][:][:, ::-1] if rev
                                            else delta[m][:]),
                                     data1=(gate[m][:][:, ::-1] if rev
                                            else gate[m][:]),
                                     initial=0.0,
                                     op0=ALU.mult, op1=ALU.add),
                                 wait=scan_w if m == 0 else [], inc="dve")
                            if rev:
                                pe_tick(6 + m, [("dve", S.cur("dve"))])
                        sc_done = [("dve", S.cur("dve"))]
                        for m in range(2):
                            c = 2 * di + m
                            hq = S.do("scalar", lambda a, c=c, m=m, htgt=htgt:
                                      a.copy(
                                          out=hh3[:, c],
                                          in_=htgt[m][:, HALO:HALO + OWN]),
                                      wait=sc_done if m == 0 else [],
                                      inc="act")
                            S.do("vector", lambda v, c=c, m=m, htgt=htgt, hq=hq:
                                 v.scalar_tensor_tensor(
                                     out=hl3[:, c], in0=hh3[:, c], scalar=-1.0,
                                     in1=htgt[m][:, HALO:HALO + OWN],
                                     op0=ALU.mult, op1=ALU.add),
                                 wait=[("act", hq)], inc="dve")
                            pe_tick(6 + m, [("dve", S.cur("dve"))])

                    if "h" in dbg:
                        tap("h", (4, 128, EXT))
                        for m in range(2):
                            S.do("sync", lambda s, m=m: s.dma_start(
                                out=taps["h"][m], in_=h_f[m][:]),
                                wait=[("dve", S.cur("dve"))],
                                inc="stap", inc_amt=16)
                            S.do("sync", lambda s, m=m: s.dma_start(
                                out=taps["h"][2 + m], in_=delta[m][:]),
                                wait=[("dve", S.cur("dve"))],
                                inc="stap", inc_amt=16)

                    S.barrier()

            S.mute = S.mute or _PHASE_LIMIT < 2
            with contextlib.ExitStack() as p4:
                def sb4(name, shape, dtype):
                    return p4.enter_context(
                        nc.sbuf_tensor(name, list(shape), dtype))

                gbg = sb4("gbg", (128, D), F32)
                gbb = sb4("gbb", (128, D), F32)
                gq1 = S.do("sync", lambda s: s.dma_start(
                    out=gbg[:], in_=gam_d[None, :].broadcast_to([128, D])),
                    inc="sgb", inc_amt=16)
                gq2 = S.do("sync", lambda s: s.dma_start(
                    out=gbb[:], in_=bet_d[None, :].broadcast_to([128, D])),
                    inc="sgb", inc_amt=16)

                xt2 = [sb4(f"xt2_{i}", (128, D), F32) for i in range(2)]
                u1t2 = [sb4(f"u1t2_{i}", (128, D), F32) for i in range(2)]
                yt = [sb4(f"yt_{i}", (128, D), F32) for i in range(2)]
                at = [sb4(f"at_{i}", (128, D), F32) for i in range(2)]
                s1t = [sb4(f"s1t_{i}", (128, D), F32) for i in range(2)]
                u2t = [sb4(f"u2t_{i}", (128, D), F32) for i in range(2)]
                st_part = sb4("st_part", (128, 8), F32)
                st_bsum = sb4("st_bsum", (128, 1), F32)
                xt2_free = [[], []]
                u1t2_free = [[], []]
                yt_free = [0, 0]
                cons_free = [0, 0]
                s1t_free = [0, 0]

                bq = S.do("scalar", lambda a: a.activation(
                    out=scratch_ref[0][:], in_=gbb[:], func=AF.Identity,
                    accum_out=st_bsum[:]),
                    wait=[("sgb", max(gq1, gq2))], inc="act")
                S.do("vector", lambda v: v.tensor_scalar_mul(
                    out=st_bsum[:], in0=st_bsum[:], scalar1=1.0 / D),
                    wait=[("act", bq)], inc="dve")
                bsum_done = S.cur("dve")
                u1all = [("su10", S.cur("su10")), ("su11", S.cur("su11"))]

                tstate = [dict() for _ in range(NT_OWN)]

                def a4_s0(j):
                    bj = j % 2
                    ts = tstate[j]
                    ts["dq"] = S.do("sync", lambda s, j=j, bj=bj: s.dma_start(
                        out=xt2[bj][:],
                        in_=xd[HALO + j * 128:HALO + (j + 1) * 128, :]),
                        wait=list(xt2_free[bj]), inc=f"sx{bj}", inc_amt=16)
                    ts["uq"] = S.do("sync", lambda s, j=j, bj=bj: s.dma_start(
                        out=u1t2[bj][:], in_=u1_d[j * 128:(j + 1) * 128, :]),
                        wait=list(u1t2_free[bj]) + (u1all if j < 2 else []),
                        inc=f"su1r{bj}", inc_amt=16)

                def a4_s1(j):
                    bj = j % 2
                    ts = tstate[j]
                    c0 = (2 * j) % NSTAT
                    c0s = slice(c0, c0 + 1)
                    ev_last = 0
                    for n in range(4):
                        bank = n
                        nmm = 0
                        q = None
                        for kc in range(4):
                            for (lh, rh) in ((hh3, wo_h3), (hh3, wo_l3),
                                             (hl3, wo_h3)):
                                nmm += 1
                                pe_w = []
                                if nmm == 1 and bank_ev[bank][0]:
                                    pe_w = [(bank_ev[bank][0], bank_ev[bank][1])]
                                q = S.do("tensor", lambda t, kc=kc, lh=lh, rh=rh,
                                         j=j, n=n, bank=bank, first=(nmm == 1),
                                         last=(nmm == 12): t.matmul(
                                             ps[bank][:],
                                             lh[:, kc, j * 128:(j + 1) * 128],
                                             rh[:, kc, n * 512:(n + 1) * 512],
                                             start=first, stop=last),
                                         wait=pe_w, inc="pe")
                        eq = S.do("vector", lambda v, n=n, bj=bj, bank=bank:
                                  v.scalar_tensor_tensor(
                                      out=yt[bj][:, n * 512:(n + 1) * 512],
                                      in0=ps[bank][:], scalar=1.0,
                                      in1=u1t2[bj][:, n * 512:(n + 1) * 512],
                                      op0=ALU.mult, op1=ALU.add,
                                      accum_out=st_part[:, 4 * bj + n:
                                                        4 * bj + n + 1]),
                                  wait=[("pe", q), (f"su1r{bj}", ts["uq"]),
                                        ("act", yt_free[bj])],
                                  inc="dve")
                        bank_ev[bank] = ("dve", eq)
                        ev_last = eq
                    u1t2_free[bj] = [("dve", ev_last)]
                    ts["sm"] = S.do("vector", lambda v, bj=bj, c0s=c0s:
                                    v.tensor_reduce(
                                        out=st_ssum[:, c0s],
                                        in_=st_part[:, 4 * bj:4 * bj + 4],
                                        axis=mybir.AxisListType.X, op=ALU.add),
                                    wait=[("dve", ev_last)], inc="dve")
                    ts["ev_last"] = ev_last

                def a4_s2(j):
                    bj = j % 2
                    ts = tstate[j]
                    c0 = (2 * j) % NSTAT
                    ln_stats(c0, yt[bj][:], wait=[("dve", ts["ev_last"])],
                             sum_done=ts["sm"])
                    ts["ych"] = S.cur("dve")

                def a4_s3(j):
                    bj = j % 2
                    ts = tstate[j]
                    c0 = (2 * j) % NSTAT
                    c1 = (2 * j + 1) % NSTAT
                    c1s = slice(c1, c1 + 1)
                    aq2 = ln_apply(c0, at[bj][:], yt[bj][:],
                                   wait_extra=[("dve", S.cur("dve"))])
                    yt_free[bj] = aq2
                    g1 = S.do("vector", lambda v, bj=bj, c1s=c1s:
                              v.scalar_tensor_tensor(
                                  out=at[bj][:], in0=at[bj][:], scalar=1.0,
                                  in1=gbg[:], op0=ALU.mult, op1=ALU.mult,
                                  accum_out=st_ssum[:, c1s]),
                              wait=[("act", aq2),
                                    ("sgb", max(gq1, gq2))], inc="dve")
                    ts["g2"] = S.do("vector", lambda v, bj=bj: v.tensor_tensor(
                        out=at[bj][:], in0=at[bj][:], in1=gbb[:], op=ALU.add),
                        wait=[("dve", g1), ("dve", bsum_done)], inc="dve")

                def a4_s4(j):
                    bj = j % 2
                    ts = tstate[j]
                    c1 = (2 * j + 1) % NSTAT
                    ln_stats(c1, at[bj][:], wait=[("dve", ts["g2"])],
                             sum_done=ts["g2"], mean_shift=st_bsum[:])

                def a4_s5(j):
                    bj = j % 2
                    ts = tstate[j]
                    c0 = (2 * j) % NSTAT
                    c0s = slice(c0, c0 + 1)
                    c1 = (2 * j + 1) % NSTAT
                    aq3 = ln_apply(c1, u2t[bj][:], at[bj][:],
                                   wait_extra=[("dve", S.cur("dve")),
                                               ("pe", cons_free[bj])])
                    s1q = S.do("vector", lambda v, bj=bj, c0s=c0s:
                               v.scalar_tensor_tensor(
                                   out=s1t[bj][:], in0=u2t[bj][:], scalar=1.0,
                                   in1=xt2[bj][:], op0=ALU.mult, op1=ALU.add,
                                   accum_out=st_ssum[:, c0s]),
                               wait=[("act", aq3),
                                     (f"sx{bj}", ts["dq"]),
                                     (f"ss1w{bj}", s1t_free[bj])],
                               inc="dve")
                    xt2_free[bj] = [("act", S.cur("act")), ("dve", s1q)]
                    s1t_free[bj] = S.do("sync", lambda s, j=j, bj=bj:
                                        s.dma_start(
                                            out=s1_d[j * 128:(j + 1) * 128, :],
                                            in_=s1t[bj][:]),
                                        wait=[("dve", s1q)],
                                        inc=f"ss1w{bj}", inc_amt=16)
                    ts["s1q"] = s1q

                def a4_s6(j):
                    bj = j % 2
                    ts = tstate[j]
                    c0 = (2 * j) % NSTAT
                    ln_stats(c0, s1t[bj][:], wait=[("dve", ts["s1q"])],
                             sum_done=ts["s1q"], newton=True)

                def a4_s7(j):
                    bj = j % 2
                    c0 = (2 * j) % NSTAT
                    tstate[j]["aq4"] = ln_apply(
                        c0, u2t[bj][:], s1t[bj][:],
                        wait_extra=[("dve", S.cur("dve"))])

                def a4_s8(j):
                    bj = j % 2
                    aq4 = tstate[j]["aq4"]
                    for f in range(16):
                        bank = (6, 7, 4, 5)[f // 4]
                        col = (f % 4) * 128
                        pe_w = [("act", aq4)] if f == 0 else []
                        if f % 4 == 0 and bank_ev[bank][0]:
                            pe_w = pe_w + [(bank_ev[bank][0], bank_ev[bank][1])]
                        pq = S.do("tensor", lambda t, f=f, bj=bj, bank=bank,
                                  col=col:
                                  t.transpose(ps[bank][:, col:col + 128],
                                              u2t[bj][:, f * 128:(f + 1) * 128],
                                              ident32[:]),
                                  wait=pe_w, inc="pe")
                        if f % 4 != 3:
                            continue
                        g = f // 4
                        cq = 0
                        for half in range(2):
                            cq = S.do("scalar", lambda a, g=g, j=j, bank=bank,
                                      half=half:
                                      a.copy(
                                          out=u2q4[:, 2 * g + half, :,
                                                   j * 128:(j + 1) * 128],
                                          in_=ps[bank][:, half * 256:
                                                       half * 256 + 256]
                                          .rearrange("p (i t) -> p i t", i=2)),
                                      wait=[("pe", pq)] if half == 0 else [],
                                      inc="act")
                        bank_ev[bank] = ("act", cq)
                    cons_free[bj] = S.cur("pe")

                stages = [a4_s0, a4_s1, a4_s2, a4_s3, a4_s4, a4_s5, a4_s6,
                          a4_s7, a4_s8]
                items = sorted(((3 * j + s, j, s)
                                for j in range(NT_OWN)
                                for s in range(len(stages))),
                               key=lambda t: ((t[1], t[2]) if _A4_SEQ
                                              else (t[0], t[1])))
                for _, j, s in items:
                    stages[s](j)

                S.barrier()

        S.mute = S.mute or _PHASE_LIMIT < 3
        with contextlib.ExitStack() as pb:
            def sbb(name, shape, dtype):
                return pb.enter_context(nc.sbuf_tensor(name, list(shape), dtype))

            a8q = sbb("a8q", (128, 32 * 2 * OWN), F8)
            a8q4 = a8q.ap().rearrange("p (kp i t) -> p kp i t", kp=32, i=2)

            with contextlib.ExitStack() as p5:
                def sb5(name, shape, dtype):
                    return p5.enter_context(
                        nc.sbuf_tensor(name, list(shape), dtype))

                NW_SLOT = 3
                wslab = [sb5(f"wslab{i}", (128, 8 * 2 * 1024), F8)
                         for i in range(NW_SLOT)]
                wslab4 = [w.ap().rearrange("p (kp i m) -> p kp i m", kp=8, i=2)
                          for w in wslab]
                slot_free = [0] * NW_SLOT
                slot_ready = [(0, 0)] * NW_SLOT
                gh_sb = [sb5(f"gh_sb{i}", (128, 4 * 1024), BF16)
                         for i in range(2)]
                fh_sb = [sb5(f"fh_sb{i}", (128, 4 * 1024), BF16)
                         for i in range(2)]
                gh_sb3 = [t.ap().rearrange("p (m t) -> p m t", m=4)
                          for t in gh_sb]
                fh_sb3 = [t.ap().rearrange("p (m t) -> p m t", m=4)
                          for t in fh_sb]
                b_1 = sb5("b_1", (128, 4 * 1024), BF16)
                b_2 = sb5("b_2", (128, 4 * 1024), F32)
                chain_done = [[], []]

                wdram = [w0_d, w1_d]
                last_cast = [0]
                casts_done = {}

                def load_slab(slot, wsel, g, wait_pe):
                    for kp in range(8):
                        src = wdram[wsel][kp * 256:(kp + 1) * 256,
                                          g * 1024:(g + 1) * 1024] \
                            .rearrange("(i p) m -> p i m", p=128)
                        half = "a" if kp < 4 else "b"
                        S.do("gpsimd", lambda gg, slot=slot, kp=kp, src=src:
                             gg.dma_start(out=wslab4[slot][:, kp], in_=src),
                             wait=wait_pe if kp == 0 else [],
                             inc=f"swl{slot}{half}", inc_amt=16)
                    return (S.cur(f"swl{slot}a"), S.cur(f"swl{slot}b"))

                slabs = [(g, w) for g in range(8) for w in range(2)]
                slot_of = {}
                for si in range(3):
                    slot_of[slabs[si]] = si % NW_SLOT
                    slot_ready[si % NW_SLOT] = load_slab(
                        si % NW_SLOT, slabs[si][1], slabs[si][0], [])
                next_load = 3

                def make_chain(g, half, gf_all):
                    gm = gh_sb[half]
                    fm = fh_sb[half]
                    base = (4 * g + 2 * half) * 2 * OWN
                    st = {}

                    def op_b1():
                        S.do("vector", lambda v: v.scalar_tensor_tensor(
                            out=b_1[:], in0=gm[:], scalar=GELU_C0,
                            in1=gm[:], op0=ALU.mult, op1=ALU.mult),
                            wait=[("act", gf_all)])

                    def op_b2():
                        st["q2"] = S.do("vector", lambda v:
                                        v.scalar_tensor_tensor(
                                            out=b_2[:], in0=b_1[:],
                                            scalar=1.0, in1=gm[:],
                                            op0=ALU.add, op1=ALU.mult))

                    def op_th():
                        st["q3"] = S.do("scalar", lambda a: a.activation(
                            out=b_2[:], in_=b_2[:],
                            func=AF.Tanh, scale=GELU_C1),
                            wait=[("dve", st["q2"])], inc="act")

                    def op_cu():
                        S.do("vector", lambda v: v.tensor_scalar(
                            out=b_2[:], in0=b_2[:],
                            scalar1=0.5, scalar2=0.5,
                            op0=ALU.mult, op1=ALU.add),
                            wait=[("act", st["q3"])])

                    def op_gv():
                        S.do("vector", lambda v: v.tensor_tensor(
                            out=gm[:], in0=b_2[:], in1=gm[:],
                            op=ALU.mult))

                    def op_ga():
                        qa = S.do("vector", lambda v: v.tensor_tensor(
                            out=gm[:], in0=gm[:], in1=fm[:],
                            op=ALU.mult))
                        chain_done[half] = [("dve", qa)]
                        st["qa"] = qa

                    def op_cast():
                        last_cast[0] = S.do(
                            "scalar", lambda a: a.copy(
                                out=a8q.ap()[:, base:base + 4 * 1024],
                                in_=gm[:]),
                            wait=[("dve", st["qa"])], inc="act")
                        casts_done[(g, half)] = last_cast[0]

                    return [op_b1, op_b2, op_th, op_cu, op_gv, op_ga,
                            op_cast]

                pending = []
                for g in range(8):
                    for half in range(2):
                        for wsel in range(2):
                            slot = slot_of[(g, wsel)]
                            tgt3 = gh_sb3[half] if wsel == 0 else fh_sb3[half]
                            for th in range(2):
                                for mloc in range(4):
                                    m = half * 4 + mloc
                                    bank = mloc
                                    pe_w = []
                                    if th == 0 and mloc == 0:
                                        pe_w.append((f"swl{slot}a",
                                                     slot_ready[slot][0]))
                                    if bank_ev[bank][0]:
                                        pe_w = pe_w + [(bank_ev[bank][0],
                                                        bank_ev[bank][1])]
                                    q = None
                                    for kp in range(8):
                                        kp_w = pe_w if kp == 0 else []
                                        if kp == 4 and th == 0 and mloc == 0:
                                            kp_w = [(f"swl{slot}b",
                                                     slot_ready[slot][1])]
                                        q = S.do("tensor", lambda t, slot=slot,
                                                 kp=kp, m=m, th=th, bank=bank,
                                                 first=(kp == 0),
                                                 last=(kp == 7): t.matmul(
                                                     ps[bank][:],
                                                     wslab4[slot][
                                                         :, kp, :,
                                                         m * 128:(m + 1) * 128],
                                                     u2q4[:, kp, :,
                                                          th * 512:
                                                          (th + 1) * 512],
                                                     start=first, stop=last,
                                                     perf_mode=PM.DoubleRow),
                                                 wait=kp_w, inc="pe")
                                    ev_w = [("pe", q)]
                                    if th == 0 and mloc == 0 and wsel == 0:
                                        ev_w.extend(chain_done[half])
                                    eq = S.do("scalar", lambda a, mloc=mloc,
                                              th=th, bank=bank, tgt3=tgt3:
                                              a.copy(
                                                  out=tgt3[:, mloc,
                                                           th * 512:
                                                           (th + 1) * 512],
                                                  in_=ps[bank][:]),
                                              wait=ev_w, inc="act")
                                    bank_ev[bank] = ("act", eq)
                                    if pending and (th * 4 + mloc) % 2 == 1:
                                        pending.pop(0)()
                            if wsel == 1:
                                while pending:
                                    pending.pop(0)()
                                pending = make_chain(g, half, S.cur("act"))
                            if half == 1:
                                slot_free[slot] = S.cur("pe")
                                if next_load < len(slabs):
                                    g2_, w2_ = slabs[next_load]
                                    slot2 = next_load % NW_SLOT
                                    slot_of[(g2_, w2_)] = slot2
                                    slot_ready[slot2] = load_slab(
                                        slot2, w2_, g2_,
                                        [("pe", slot_free[slot2])])
                                    next_load += 1
                for kp in range(8):
                    srcw = w2_d[kp * 256:(kp + 1) * 256, 0:512] \
                        .rearrange("(i p) m -> p i m", p=128)
                    S.do("gpsimd", lambda gg, kp=kp, srcw=srcw: gg.dma_start(
                        out=w2pre4[:, kp], in_=srcw),
                        inc="sw2p", inc_amt=16)
                w2pre_done = S.cur("sw2p")
                for emit in pending:
                    emit()
                b1_pe = S.cur("pe")
                b1_act = S.cur("act")
                b1_dve = S.cur("dve")
            u2q_stack.close()
            S.mute = S.mute or _PHASE_LIMIT < 4

            if "a8" in dbg:
                tap("a8", (128, 32 * 2 * OWN), F8)
                S.do("sync", lambda s: s.dma_start(out=taps["a8"][:],
                                                   in_=a8q[:]),
                     inc="stap", inc_amt=16)

            with contextlib.ExitStack() as p6:
                def sb6(name, shape, dtype):
                    return p6.enter_context(
                        nc.sbuf_tensor(name, list(shape), dtype))

                w2slab = [sb6(f"w2slab{i}", (128, 32 * 2 * 512), F8)
                          for i in range(2)]
                w2slab4 = [w.ap().rearrange("p (kp i n) -> p kp i n",
                                            kp=32, i=2) for w in w2slab]
                w2_free = [0, 0]
                w2_ready = [[0] * 4, [0] * 4]
                fo = sb6("fo", (128, 8 * D), BF16)
                fo3 = fo.ap().rearrange("p (t d) -> p t d", t=8)
                fo_sum = sb6("fo_sum", (128, 32), F32)
                fo_sum4 = fo_sum.ap().rearrange("p (t n) -> p t n", t=8)
                fo_ssq = sb6("fo_ssq", (128, 32), F32)
                fo_ssq4 = fo_ssq.ap().rearrange("p (t n) -> p t n", t=8)
                fo_ready = [0] * 8

                s1b = [sb6(f"s1b_{i}", (128, D), F32) for i in range(2)]
                lnf = [sb6(f"lnf_{i}", (128, D), BF16) for i in range(2)]
                s1b_free = [0, 0]
                s1w_all = [("ss1w0", S.cur("ss1w0")), ("ss1w1", S.cur("ss1w1")),
                           ("act", b1_act), ("dve", b1_dve)]

                def finale(j):
                    bj = j % 2
                    c0 = j % NSTAT
                    cs = slice(c0, c0 + 1)
                    dq = S.do("sync", lambda s, j=j, bj=bj: s.dma_start(
                        out=s1b[bj][:], in_=s1_d[j * 128:(j + 1) * 128, :]),
                        wait=[(f"sto{bj}", s1b_free[bj])] + s1w_all,
                        inc=f"sb{bj}", inc_amt=16)
                    S.do("vector", lambda v, j=j, cs=cs: v.tensor_reduce(
                        out=st_ssum[:, cs], in_=fo_sum4[:, j, :],
                        axis=mybir.AxisListType.X, op=ALU.add),
                        wait=[("act", fo_ready[j])], inc="dve")
                    S.do("vector", lambda v, j=j, cs=cs: v.tensor_reduce(
                        out=st_ssq[:, cs], in_=fo_ssq4[:, j, :],
                        axis=mybir.AxisListType.X, op=ALU.add), inc="dve")
                    ln_stats_post(c0, d=D, newton=True)
                    aq = ln_apply(c0, lnf[bj][:], fo3[:, j, :],
                                  wait_extra=[("dve", S.cur("dve"))])
                    sq = S.do("vector", lambda v, bj=bj: v.tensor_tensor(
                        out=s1b[bj][:], in0=lnf[bj][:], in1=s1b[bj][:],
                        op=ALU.add),
                        wait=[("act", aq), (f"sb{bj}", dq)], inc="dve")
                    s1b_free[bj] = S.do("sync", lambda s, j=j, bj=bj:
                                        s.dma_start(
                                            out=yd[j * 128:(j + 1) * 128, :],
                                            in_=s1b[bj][:]),
                                        wait=[("dve", sq)],
                                        inc=f"sto{bj}", inc_amt=16)

                def load_w2(slot, n, wait_pe, kp0=0):
                    for kp in range(kp0, 32):
                        src = w2_d[kp * 256:(kp + 1) * 256,
                                   n * 512:(n + 1) * 512] \
                            .rearrange("(i p) m -> p i m", p=128)
                        sub = kp // 8
                        S.do("gpsimd", lambda gg, slot=slot, kp=kp, src=src:
                             gg.dma_start(out=w2slab4[slot][:, kp], in_=src),
                             wait=wait_pe if kp == kp0 else [],
                             inc=f"sw2{slot}{sub}", inc_amt=16)
                        w2_ready[slot][sub] = S.cur(f"sw2{slot}{sub}")
                    return None

                load_w2(0, 0, [("pe", b1_pe)], kp0=8)
                for n in range(4):
                    slot = n % 2
                    if n + 1 < 4:
                        nslot = (n + 1) % 2
                        load_w2(nslot, n + 1,
                                [("pe", w2_free[nslot])] +
                                ([("act", b1_act), ("dve", b1_dve)]
                                 if n == 0 else []))
                    for t in range(8):
                        bank = 4 + t % 2
                        pe_w = []
                        if bank_ev[bank][0]:
                            pe_w = pe_w + [(bank_ev[bank][0], bank_ev[bank][1])]
                        q = None
                        for kp in range(32):
                            kp_w = list(pe_w) if kp == 0 else []
                            use_pre = (n == 0 and kp < 8)
                            if n == 0 and t == 0 and kp % 2 == 0:
                                kp_w.append(("act", casts_done[
                                    (kp // 4, (kp % 4) // 2)]))
                            if t == 0 and kp % 8 == 0:
                                if use_pre:
                                    kp_w.append(("sw2p", w2pre_done))
                                else:
                                    kp_w.append((f"sw2{slot}{kp // 8}",
                                                 w2_ready[slot][kp // 8]))
                            rhs_t = (w2pre4[:, kp] if use_pre
                                     else w2slab4[slot][:, kp])
                            q = S.do("tensor", lambda tt, rhs_t=rhs_t, kp=kp,
                                     t=t, bank=bank, first=(kp == 0),
                                     last=(kp == 31): tt.matmul(
                                         ps[bank][:],
                                         a8q4[:, kp, :, t * 128:(t + 1) * 128],
                                         rhs_t,
                                         start=first, stop=last,
                                         perf_mode=PM.DoubleRow),
                                     wait=kp_w, inc="pe")
                        eq = S.do("scalar", lambda a, t=t, n=n, bank=bank:
                                  a.activation(
                                      out=fo3[:, t, n * 512:(n + 1) * 512],
                                      in_=ps[bank][:], func=AF.Identity,
                                      accum_out=fo_sum4[:, t, n:n + 1]),
                                  wait=[("pe", q)], inc="act")
                        sqq = S.do("scalar", lambda a, t=t, n=n: a.activation(
                            out=scratch_ref[0][:, 0:512],
                            in_=fo3[:, t, n * 512:(n + 1) * 512],
                            func=AF.Square,
                            accum_out=fo_ssq4[:, t, n:n + 1]), inc="act")
                        bank_ev[bank] = ("act", eq)
                        if n == 3:
                            fo_ready[t] = sqq
                            finale(t)
                    w2_free[slot] = S.cur("pe")

                S.wait_only("sync", [("sto0", S.cur("sto0")),
                                     ("sto1", S.cur("sto1")),
                                     ("stap", S.cur("stap"))])

        with nc.Block() as block:
            @block.sync
            def _(sync):
                for f in S.q["sync"]:
                    f(sync)

            @block.scalar
            def _(scalar):
                for f in S.q["scalar"]:
                    f(scalar)

            @block.vector
            def _(vector):
                for f in S.q["vector"]:
                    f(vector)

            @block.gpsimd
            def _(gpsimd):
                for f in S.q["gpsimd"]:
                    f(gpsimd)

            @block.tensor
            def _(tensor):
                for f in S.q["tensor"]:
                    f(tensor)

    return nc


def _shard_inputs(inputs):
    x = np.ascontiguousarray(np.asarray(inputs["L0_attn_x"], dtype=np.float32)
                             .reshape(B * L, D))
    w_in = np.ascontiguousarray(np.asarray(inputs["L0_attn_W_in"], np.float32))
    w_fwd = np.ascontiguousarray(np.asarray(inputs["L0_attn_W_fwd"], np.float32))
    w_bwd = np.ascontiguousarray(np.asarray(inputs["L0_attn_W_bwd"], np.float32))
    w_out = np.ascontiguousarray(np.asarray(inputs["L0_attn_W_out"], np.float32))
    gamma = np.ascontiguousarray(np.asarray(inputs["L0_attn_gamma"], np.float32))
    beta = np.ascontiguousarray(np.asarray(inputs["L0_attn_beta"], np.float32))
    wg = np.asarray(inputs["L0_ffn_w_gating"], np.float32)
    w0 = np.ascontiguousarray(wg[0])
    w1 = np.ascontiguousarray(wg[1])
    w2 = np.ascontiguousarray(np.asarray(inputs["L0_ffn_w_linear"], np.float32))
    zeros = np.zeros((HALO, D), np.float32)
    in_maps = []
    for c in range(8):
        s, h = divmod(c, 2)
        base = s * L + h * OWN
        own = x[base:base + OWN]
        pre = x[base - HALO:base] if h == 1 else zeros
        post = x[base + OWN:base + OWN + HALO] if h == 0 else zeros
        xe = np.ascontiguousarray(np.concatenate([pre, own, post], axis=0))
        in_maps.append({
            "x": xe, "w_in": w_in, "w_fwd": w_fwd, "w_bwd": w_bwd,
            "w_out": w_out, "gamma": gamma, "beta": beta,
            "w0": w0, "w1": w1, "w2": w2,
        })
    return in_maps


def kernel(**inputs):
    in_maps = _shard_inputs(inputs)[:_N_CORES]
    nc = build(debug_taps=_DEBUG_TAPS)
    res = run_bass_kernel_spmd(nc, in_maps, core_ids=list(range(_N_CORES)))
    out = np.zeros((B * L, D), np.float32)
    for c in range(_N_CORES):
        s, h = divmod(c, 2)
        base = s * L + h * OWN
        out[base:base + OWN] = res.results[c]["y"]
    if _DEBUG_TAPS:
        kernel._last_results = res.results
    return out.reshape(B, L, D)
